# version 40
# baseline (speedup 1.0000x reference)
import numpy as np
import ml_dtypes
import concourse.bass as bass
import concourse.mybir as mybir
from concourse.bass_utils import run_bass_kernel_spmd
from concourse.alu_op_type import AluOpType
from contextlib import ExitStack

F32 = mybir.dt.float32
BF16 = mybir.dt.bfloat16
AF = mybir.ActivationFunctionType
AX = mybir.AxisListType

H = 1024
FC = H // 128
NPAIR = 8
NE = 8
LN_EPS = 1e-5

ENG_SEM = {"tensor": "tp", "vector": "tv", "scalar": "ta", "gpsimd": "tg", "sync": "ts"}
SELF_SYNC_ENGS = ("vector", "scalar", "gpsimd")
ALL_ENGS = ("tensor", "vector", "scalar", "gpsimd", "sync")


class Sched:

    def __init__(self, semnames):
        self.q = {e: [] for e in ALL_ENGS}
        self.cnt = {s: 0 for s in semnames}
        self.known = {e: {} for e in ALL_ENGS}
        self.last_implies = {e: {} for e in ALL_ENGS}
        self.writer = {}
        self.readers = {}
        self.cuts = []

    def _need(self, eng, dep):
        deng, sem, val, implies = dep
        if deng == eng and eng not in SELF_SYNC_ENGS:
            return
        k = self.known[eng]
        if k.get(sem, 0) >= val:
            return
        self.q[eng].append(("wait", sem, val))
        for s2, v2 in implies.items():
            if k.get(s2, 0) < v2:
                k[s2] = v2
        if k.get(sem, 0) < val:
            k[sem] = val

    def op(self, eng, builders, reads=(), writes=(), sem=None, amt=1):
        sem = sem or ENG_SEM[eng]
        for r in reads:
            d = self.writer.get(r)
            if d is not None:
                self._need(eng, d)
        for w in writes:
            d = self.writer.get(w)
            if d is not None:
                self._need(eng, d)
            for dep in list(self.readers.get(w, {}).values()):
                self._need(eng, dep)
        self.q[eng].append(("ops", builders, sem, amt))
        self.cnt[sem] += amt
        val = self.cnt[sem]
        if sem not in ENG_SEM.values():
            implies = dict(self.last_implies[eng])
            implies.update(self.known[eng])
            implies[sem] = val
            dep = ("dma", sem, val, implies)
            for r in reads:
                self.readers.setdefault(r, {})[sem] = dep
            for w in writes:
                self.writer[w] = dep
                self.readers[w] = {}
            return dep
        implies = dict(self.last_implies[eng])
        implies.update(self.known[eng])
        implies[sem] = val
        self.last_implies[eng] = implies
        dep = (eng, sem, val, implies)
        for r in reads:
            self.readers.setdefault(r, {})[sem] = dep
        for w in writes:
            self.writer[w] = dep
            self.readers[w] = {}
        return dep

    def wait_raw(self, eng, sem, val):
        if self.known[eng].get(sem, 0) < val:
            self.q[eng].append(("wait", sem, val))
            self.known[eng][sem] = val

    def cut(self):
        self.cuts.append({e: len(self.q[e]) for e in ALL_ENGS})
        done = {s: self.cnt[s] for s in self.cnt if s not in ("dxa", "dx0", "dy", "dy2")}
        for e in ALL_ENGS:
            k = self.known[e]
            for s, v in done.items():
                if k.get(s, 0) < v:
                    k[s] = v

    def replay(self, nc, sems, seg, block):
        starts = self.cuts[seg - 1] if seg > 0 else {e: 0 for e in ALL_ENGS}
        ends = self.cuts[seg] if seg < len(self.cuts) else {e: len(self.q[e]) for e in ALL_ENGS}

        def run(engname, handle):
            for item in self.q[engname][starts[engname]:ends[engname]]:
                if item[0] == "wait":
                    handle.wait_ge(sems[item[1]], item[2])
                else:
                    _, builders, sem, amt = item
                    ins = None
                    for b in builders:
                        ins = b(handle)
                    ins.then_inc(sems[sem], amt)

        @block.tensor
        def _(t):
            run("tensor", t)

        @block.vector
        def _(v):
            run("vector", v)

        @block.scalar
        def _(s):
            run("scalar", s)

        @block.gpsimd
        def _(g):
            run("gpsimd", g)

        @block.sync
        def _(sy):
            run("sync", sy)


def build_nc(cfg):
    T = cfg["T"]; TAU = cfg["TAU"]; DFF = cfg["DFF"]; SLAB = cfg["SLAB"]
    NT = T // TAU
    NTC = T // 128
    NG = NTC // 2
    NSL = DFF // SLAB
    NMC = SLAB // 128
    NKD = SLAB // 128
    assert TAU <= 512 and T % TAU == 0 and T % 256 == 0 and SLAB % 128 == 0

    nc = bass.Bass("TRN2", target_bir_lowering=False)

    dx_in = nc.dram_tensor("x_in", [H, T], F32, kind="ExternalInput").ap()
    dwqkv = nc.dram_tensor("wqkv", [H, 3 * H], F32, kind="ExternalInput").ap()
    dbqkv = nc.dram_tensor("bqkv", [128, 24], F32, kind="ExternalInput").ap()
    dwout = nc.dram_tensor("wout", [H, H], F32, kind="ExternalInput").ap()
    dbout = nc.dram_tensor("bout", [128, FC], F32, kind="ExternalInput").ap()
    dwr = nc.dram_tensor("wr", [128, FC, NE], F32, kind="ExternalInput").ap()
    dwg = nc.dram_tensor("wg", [NE, H, DFF], F32, kind="ExternalInput").ap()
    dwu = nc.dram_tensor("wu", [NE, H, DFF], F32, kind="ExternalInput").ap()
    dwd = nc.dram_tensor("wd", [NE, DFF, H], F32, kind="ExternalInput").ap()
    dsg = nc.dram_tensor("sg", [H, DFF], F32, kind="ExternalInput").ap()
    dsu = nc.dram_tensor("su", [H, DFF], F32, kind="ExternalInput").ap()
    dsd = nc.dram_tensor("sd", [DFF, H], F32, kind="ExternalInput").ap()
    deye = nc.dram_tensor("eye", [128, 128], F32, kind="ExternalInput").ap()
    deyeb = nc.dram_tensor("eyeb", [128, 128], BF16, kind="ExternalInput").ap()
    d1row = nc.dram_tensor("onesrow", [1, 128], F32, kind="ExternalInput").ap()
    d1col = nc.dram_tensor("onescol", [128, 1], BF16, kind="ExternalInput").ap()
    d1rowb = nc.dram_tensor("onesrowb", [1, 128], BF16, kind="ExternalInput").ap()
    dy = nc.dram_tensor("y", [H, T], F32, kind="ExternalOutput").ap()
    DBG = cfg.get("dbg", False)
    if DBG:
        d_x1 = nc.dram_tensor("dbg_x1", [128, FC, T], F32, kind="ExternalOutput").ap()
        d_xn = nc.dram_tensor("dbg_xn", [128, FC, T], BF16, kind="ExternalOutput").ap()
        d_m = nc.dram_tensor("dbg_m", [128, T // 128, NE], F32, kind="ExternalOutput").ap()

    semnames = ["tp", "tv", "ta", "tg", "ts", "dxa", "dx0", "dy", "dy2",
                "dwq0", "dwq1", "dwo", "dwf0", "dwf1", "dwf2", "dbg"]
    sc = Sched(semnames)

    es0 = ExitStack()
    sb = lambda name, shape, dt: es0.enter_context(nc.sbuf_tensor(name, shape, dt))
    x = sb("x", [128, FC, T], F32)
    xn = sb("xn", [128, FC, T], BF16)
    eye = sb("eyes", [128, 128], F32)
    eyeb = sb("eyebs", [128, 128], BF16)
    m_tmb = sb("m_tmb", [128, NTC, NE], BF16)
    onesrow = sb("onesrows", [1, 128], F32)
    onescol = sb("onescols", [128, 1], BF16)
    onesrowb = sb("onesrowbs", [1, 128], BF16)
    bqkv = sb("bqkvs", [128, 24], F32)
    bqs = sb("bqss", [128, 8], F32)
    bout = sb("bouts", [128, FC], F32)
    wr_sb = sb("wrs", [128, FC, NE], BF16)
    xb = sb("xb", [128, 2, TAU], BF16)
    xsq = sb("xsq", [128, 2, TAU], BF16)
    s_mu = sb("s_mu", [1, TAU], F32)
    s_ex2 = sb("s_ex2", [1, TAU], F32)
    s_var = sb("s_var", [1, TAU], F32)
    s_rstd = sb("s_rstd", [1, TAU], F32)
    s_std = s_ex2
    s_bln = s_mu
    lnscr = sb("lnscr", [128, TAU], F32)
    epsc = sb("epsc", [1, 1], F32)
    m_rep = sb("m_rep", [128, 2, T], F32)
    r_e = sb("r_e", [128, NE], F32)
    r_t1 = sb("r_t1", [128, NE], F32)
    r_m = sb("r_m", [128, 4], F32)
    mrow = sb("mrow", [1, 512], BF16)

    sems = {n: es0.enter_context(nc.semaphore(n)) for n in semnames}
    seg_idx = [0]

    def flush_phase():
        sc.cut()
        with nc.Block() as block:
            sc.replay(nc, sems, seg_idx[0], block)
        seg_idx[0] += 1

    def dma_group(queue_eng, dmasem, items):
        allw = []
        dep = None
        for dst, src, reads, writes in items:
            dep = sc.op(queue_eng, [lambda e, d=dst, s=src: e.dma_start(out=d, in_=s)],
                        reads=reads, writes=writes, sem=dmasem, amt=16)
            allw += list(writes)
        for r in allw:
            sc.writer[r] = dep
        return dep

    def dma(queue_eng, dmasem, dst, src, reads=(), writes=()):
        return dma_group(queue_eng, dmasem, [(dst, src, reads, writes)])

    def ln(which):
        inv_h = 1.0 / H

        def sums(t):
            tp_ = t % 2
            xres = [f"x{f}_{t}" for f in range(FC)]
            for f in range(FC):
                par = f % 2
                cast_eng = "gpsimd" if f < 4 else "scalar"
                if cast_eng == "gpsimd":
                    sc.op("gpsimd", [lambda e, f=f, t=t, par=par: e.tensor_copy(
                        out=xb[:, par, :], in_=x[:, f, t * TAU:(t + 1) * TAU])],
                        reads=[xres[f]], writes=[f"xb{par}"])
                else:
                    sc.op("scalar", [lambda e, f=f, t=t, par=par: e.copy(
                        out=xb[:, par, :], in_=x[:, f, t * TAU:(t + 1) * TAU])],
                        reads=[xres[f]], writes=[f"xb{par}"])
                sc.op("scalar", [lambda e, f=f, t=t, par=par: e.activation(
                    out=xsq[:, par, :], in_=x[:, f, t * TAU:(t + 1) * TAU], func=AF.Square)],
                    reads=[xres[f]], writes=[f"xsq{par}"])
                sc.op("tensor", [lambda e, f=f, par=par, tp_=tp_: e.matmul(
                    out=pS[tp_][0:1, 0, 0:TAU], lhsT=onescol[:], rhs=xb[:, par, :],
                    start=(f == 0), stop=(f == FC - 1))],
                    reads=[f"xb{par}", "consts"], writes=[f"bS{tp_}0"])
                sc.op("tensor", [lambda e, f=f, par=par, tp_=tp_: e.matmul(
                    out=pS[tp_][0:1, 1, 0:TAU], lhsT=onescol[:], rhs=xsq[:, par, :],
                    start=(f == 0), stop=(f == FC - 1))],
                    reads=[f"xsq{par}", "consts"], writes=[f"bS{tp_}1"])

        def rest(t):
            tp_ = t % 2
            s1r, s2r = f"bS{tp_}0", f"bS{tp_}1"
            xres = [f"x{f}_{t}" for f in range(FC)]
            sc.op("vector", [lambda e, tp_=tp_: e.tensor_scalar(out=s_mu[:],
                  in0=pS[tp_][0:1, 0, 0:TAU],
                  scalar1=inv_h, scalar2=None, op0=AluOpType.mult)],
                  reads=[s1r], writes=["s_mu"])
            sc.op("vector", [lambda e, tp_=tp_: e.tensor_scalar(out=s_ex2[:],
                  in0=pS[tp_][0:1, 1, 0:TAU],
                  scalar1=inv_h, scalar2=None, op0=AluOpType.mult)],
                  reads=[s2r], writes=["s_ex2"])
            sc.op("vector", [lambda e: e.tensor_tensor(out=s_var[:], in0=s_mu[:], in1=s_mu[:],
                  op=AluOpType.mult)], reads=["s_mu"], writes=["s_var"])
            sc.op("vector", [lambda e: e.tensor_tensor(out=s_var[:], in0=s_ex2[:], in1=s_var[:],
                  op=AluOpType.subtract)], reads=["s_ex2", "s_var"], writes=["s_var"])
            sc.op("scalar", [lambda e: e.activation(out=s_std[:], in_=s_var[:], func=AF.Sqrt,
                  bias=epsc[0:1, 0:1])], reads=["s_var", "epsc", "s_ex2"], writes=["s_ex2"])
            sc.op("vector", [lambda e: e.reciprocal(out=s_rstd[:], in_=s_std[:])],
                  reads=["s_ex2"], writes=["s_rstd"])
            sc.op("vector", [lambda e: e.scalar_tensor_tensor(out=s_bln[:], in0=s_mu[:],
                  scalar=-1.0, in1=s_rstd[:], op0=AluOpType.mult, op1=AluOpType.mult)],
                  reads=["s_mu", "s_rstd"], writes=["s_mu"])
            sc.op("tensor", [lambda e: e.matmul(out=po[0][:, 0:TAU], lhsT=onesrow[:],
                  rhs=s_rstd[:], start=True, stop=True)],
                  reads=["s_rstd", "consts"], writes=["po0"])
            sc.op("tensor", [lambda e: e.matmul(out=po[1][:, 0:TAU], lhsT=onesrow[:],
                  rhs=s_bln[:], start=True, stop=True)],
                  reads=["s_mu", "consts"], writes=["po1"])
            sc.op("vector", [lambda e: e.tensor_copy(out=a_sb[:], in_=po[0][:, 0:TAU])],
                  reads=["po0"], writes=["a_sb"])
            sc.op("vector", [lambda e: e.tensor_copy(out=b_sb[:], in_=po[1][:, 0:TAU])],
                  reads=["po1"], writes=["b_sb"])
            for f in range(FC):
                if f < 5:
                    sc.op("vector", [lambda e, f=f, t=t: e.tensor_tensor(
                        out=lnscr[:], in0=x[:, f, t * TAU:(t + 1) * TAU], in1=po[0][:, 0:TAU],
                        op=AluOpType.mult)], reads=[xres[f], "po0"], writes=["lnscr"])
                    sc.op("vector", [lambda e, f=f, t=t: e.tensor_tensor(
                        out=xn[:, f, t * TAU:(t + 1) * TAU], in0=lnscr[:], in1=po[1][:, 0:TAU],
                        op=AluOpType.add)], reads=["lnscr", "po1"], writes=[f"xn{f}_{t}"])
                else:
                    sc.op("gpsimd", [lambda e, f=f, t=t: e.tensor_tensor(
                        out=xn[:, f, t * TAU:(t + 1) * TAU],
                        in0=x[:, f, t * TAU:(t + 1) * TAU], in1=a_sb[:],
                        op=AluOpType.mult)],
                        reads=[xres[f], "a_sb"], writes=[f"xn{f}_{t}"])
                    sc.op("gpsimd", [lambda e, f=f, t=t: e.tensor_tensor(
                        out=xn[:, f, t * TAU:(t + 1) * TAU],
                        in0=xn[:, f, t * TAU:(t + 1) * TAU], in1=b_sb[:],
                        op=AluOpType.add)],
                        reads=[f"xn{f}_{t}", "b_sb"], writes=[f"xn{f}_{t}"])

        sums(0)
        for t in range(NT):
            if t + 1 < NT:
                sums(t + 1)
            rest(t)

    esB = ExitStack()
    sbB = lambda name, shape, dt: esB.enter_context(nc.sbuf_tensor(name, shape, dt))
    wq_sb = sbB("wq", [128, 2, FC, 384], BF16)
    wout_sb = sbB("wouts", [128, FC, H], BF16)
    Q = sbB("Qs", [128, 2, T], BF16)
    Kt = sbB("Kts", [128, 2, T], BF16)
    Va = sbB("Va", [128, 2, 2, NTC, 65], BF16)
    PT = sbB("PTs", [128, 6, 2 * TAU], BF16)
    O_sb = sbB("Os", [128, 2, T], BF16)
    otmp = sbB("otmp", [64, 2, TAU], BF16)
    onrm = otmp
    rz = sbB("rz", [1, 2, TAU], BF16)
    kb = sbB("kb", [128, 4], F32)
    a_sb = sbB("a_sb", [128, TAU], BF16)
    b_sb = sbB("b_sb", [128, TAU], BF16)

    pS = [esB.enter_context(nc.psum_tensor(f"pS{i}", [128, 2, 512], F32))
          for i in range(3)]
    po0 = esB.enter_context(nc.psum_tensor("po0", [128, 512], F32))
    po1 = esB.enter_context(nc.psum_tensor("po1", [128, 512], F32))
    po = [po0, po1]

    dxr = dx_in.rearrange("(fc p) t -> p fc t", p=128)
    dma_group("sync", "dx0", [
        (x[:, :, 0:TAU], dxr[:, :, 0:TAU], (), [f"x{f}_0" for f in range(FC)]),
        (eye[:], deye, (), ["consts_eye"]),
        (eyeb[:], deyeb, (), ["consts_eye"]),
        (onesrow[:], d1row, (), ["consts"]),
        (onescol[:], d1col, (), ["consts"]),
        (onesrowb[:], d1rowb, (), ["consts"]),
        (bqkv[:], dbqkv, (), ["bqkv"]),
        (bout[:], dbout, (), ["bout"]),
    ])
    if NT > 1:
        dma_group("sync", "dxa", [
            (x[:, :, TAU:T], dxr[:, :, TAU:T], (),
             [f"x{f}_{t}" for f in range(FC) for t in range(1, NT)])])
    dma("gpsimd", "dwo", wr_sb[:], dwr, writes=["wr"])
    sc.op("gpsimd", [lambda e: e.memset(epsc[:], LN_EPS)], writes=["epsc"])
    sc.op("vector", [lambda e: e.tensor_scalar(out=bqs[:], in0=bqkv[:, 0:24:3],
          scalar1=0.125, scalar2=None, op0=AluOpType.mult)],
          reads=["bqkv"], writes=["bqs"])

    ln("ln1")


    for f in range(FC):
        sc.op("vector", [lambda e, f=f: e.tensor_scalar(out=x[:, f, :], in0=x[:, f, :],
              scalar1=bout[:, f:f + 1], scalar2=None, op0=AluOpType.add)],
              reads=["bout"] + [f"x{f}_{t}" for t in range(NT)],
              writes=[f"x{f}_{t}" for t in range(NT)])

    sc.op("gpsimd", [lambda e: e.memset(Va[:, 0, 0, :, 64:65], 1.0),
                     lambda e: e.memset(Va[:, 0, 1, :, 64:65], 1.0),
                     lambda e: e.memset(Va[:, 1, 0, :, 64:65], 1.0),
                     lambda e: e.memset(Va[:, 1, 1, :, 64:65], 1.0)],
          writes=["Va_ones"])

    dma("gpsimd", "dwq0", wq_sb[:, 0, :, :],
        dwqkv[:, 0:384].rearrange("(kc p) c -> p kc c", p=128), writes=["wq0"])

    pool_i = [0]

    def take_pair():
        i = pool_i[0]
        pool_i[0] = (i + 1) % 3
        return i

    def gen_qkv(hp):
        wpar = hp % 2
        for t in range(NT):
            for part, coloff in (("q", 0), ("k", 128)):
                pi = take_pair()
                bank = 0
                pt = pS[pi]
                res = f"bS{pi}0"
                builders = []
                for kc in range(FC):
                    builders.append(lambda e, kc=kc, wpar=wpar, coloff=coloff, t=t,
                                    bank=bank, pt=pt: e.matmul(
                        out=pt[:, bank, 0:TAU],
                        lhsT=wq_sb[:, wpar, kc, coloff:coloff + 128],
                        rhs=xn[:, kc, t * TAU:(t + 1) * TAU],
                        start=(kc == 0), stop=(kc == FC - 1)))
                sc.op("tensor", builders,
                      reads=[f"xn{kc}_{t}" for kc in range(FC)] + [f"wq{wpar}"],
                      writes=[res])
                if part == "q":
                    sc.op("vector", [lambda e, t=t, pt=pt, hp=hp, wpar=wpar, bank=bank:
                          e.tensor_scalar(
                        out=Q[:, wpar, t * TAU:(t + 1) * TAU], in0=pt[:, bank, 0:TAU],
                        scalar1=0.125, scalar2=bqs[:, hp:hp + 1],
                        op0=AluOpType.mult, op1=AluOpType.add)],
                        reads=[res, "bqs"], writes=[f"Q{wpar}_{t}"])
                else:
                    sc.op("vector", [lambda e, t=t, pt=pt, hp=hp, wpar=wpar, bank=bank:
                          e.tensor_scalar(
                        out=Kt[:, wpar, t * TAU:(t + 1) * TAU], in0=pt[:, bank, 0:TAU],
                        scalar1=bqkv[:, 3 * hp + 1:3 * hp + 2], scalar2=None,
                        op0=AluOpType.add)],
                        reads=[res, "bqkv"], writes=[f"K{wpar}_{t}"])
                yield
        for tc in range(NTC):
            pi = take_pair()
            bank = 0
            pt = pS[pi]
            res = f"bS{pi}0"
            builders = []
            for kc in range(FC):
                builders.append(lambda e, kc=kc, tc=tc, wpar=wpar, bank=bank, pt=pt:
                                e.matmul(
                    out=pt[:, bank, 0:128], lhsT=xn[:, kc, tc * 128:(tc + 1) * 128],
                    rhs=wq_sb[:, wpar, kc, 256:384],
                    start=(kc == 0), stop=(kc == FC - 1)))
            sc.op("tensor", builders,
                  reads=[f"xn{kc}_{tc * 128 // TAU}" for kc in range(FC)] + [f"wq{wpar}"],
                  writes=[res])
            sc.op("vector", [lambda e, tc=tc, pt=pt, wpar=wpar, bank=bank: e.tensor_copy(
                out=Va[:, wpar, 0, tc, 0:64], in_=pt[:, bank, 0:64])],
                reads=[res], writes=[f"Va{wpar}_{tc}"])
            sc.op("vector", [lambda e, tc=tc, pt=pt, wpar=wpar, bank=bank: e.tensor_copy(
                out=Va[:, wpar, 1, tc, 0:64], in_=pt[:, bank, 64:128])],
                reads=[res], writes=[f"Va{wpar}_{tc}"])
            yield
        kreads = [f"K{wpar}_{t}" for t in range(NT)]
        sc.op("vector", [lambda e, wpar=wpar: e.reduce_sum(
            out=kb[:, 2 * wpar:2 * wpar + 1], in_=Kt[:, wpar, :], axis=AX.X)],
            reads=kreads, writes=[f"kb{wpar}"])
        sc.op("vector", [lambda e, wpar=wpar: e.tensor_scalar(
            out=kb[:, 2 * wpar + 1:2 * wpar + 2], in0=kb[:, 2 * wpar:2 * wpar + 1],
            scalar1=1.0 / T, scalar2=None, op0=AluOpType.mult)],
            reads=[f"kb{wpar}"], writes=[f"kbs{wpar}"])
        sc.op("vector", [lambda e, wpar=wpar: e.tensor_scalar(
            out=Kt[:, wpar, :], in0=Kt[:, wpar, :],
            scalar1=kb[:, 2 * wpar + 1:2 * wpar + 2], scalar2=None,
            op0=AluOpType.subtract)],
            reads=[f"kbs{wpar}"] + kreads, writes=kreads)

    for _ in gen_qkv(0):
        pass

    for hp in range(NPAIR):
        wpar = hp % 2
        if hp + 1 < NPAIR:
            dma("gpsimd", f"dwq{(hp + 1) % 2}", wq_sb[:, (hp + 1) % 2, :, :],
                dwqkv[:, (hp + 1) * 384:(hp + 2) * 384].rearrange("(kc p) c -> p kc c", p=128),
                writes=[f"wq{(hp + 1) % 2}"])
            nxt = gen_qkv(hp + 1)
        else:
            nxt = None
        if hp == 0:
            dma("gpsimd", "dwo", wout_sb[:],
                dwout.rearrange("(rc p) c -> p rc c", p=128), writes=["wout"])

        pending_norm = []

        def flush_norms():
            for fn in pending_norm:
                fn()
            pending_norm.clear()

        for qb in range(NT):
            def emit_S2(g):
                pa = take_pair()
                pb = take_pair()
                sg2p[g] = (pa, pb)
                builders = []
                sres = []
                for h, pi in ((0, pa), (1, pb)):
                    hb = 64 * h
                    for j in range(2):
                        kc = 2 * g + j
                        builders.append(lambda e, kc=kc, j=j, hb=hb, qb=qb, pi=pi,
                                        wpar=wpar: e.matmul(
                            out=pS[pi][:, j, 0:TAU],
                            lhsT=Kt[:, wpar, kc * 128:(kc + 1) * 128][hb:hb + 64, :],
                            rhs=Q[:, wpar, qb * TAU:(qb + 1) * TAU][hb:hb + 64, :],
                            start=True, stop=True))
                    sres += [f"bS{pi}0", f"bS{pi}1"]
                kcr = [f"K{wpar}_{(2 * g) * 128 // TAU}",
                       f"K{wpar}_{(2 * g + 1) * 128 // TAU}", f"Q{wpar}_{qb}"]
                sc.op("tensor", builders, reads=kcr, writes=sres)

            def emit_exp(g, h):
                pi = sg2p[g][h]
                pti = (g % 3) * 2 + h
                sc.op("scalar",
                      [lambda e, pi=pi, pti=pti: e.activation(
                          out=PT[:, pti, :], in_=pS[pi][:, :, 0:TAU], func=AF.Exp)],
                      reads=[f"bS{pi}0", f"bS{pi}1"], writes=[f"PT{pti}"])

            def emit_PV(g, h):
                pti = (g % 3) * 2 + h
                pvb = []
                for j in range(2):
                    kc = 2 * g + j
                    pvb.append(lambda e, kc=kc, j=j, h=h, g=g, pti=pti,
                               wpar=wpar: e.matmul(
                        out=po[h][0:65, 0:TAU], lhsT=Va[:, wpar, h, kc, :],
                        rhs=PT[:, pti, j * TAU:(j + 1) * TAU],
                        start=(g == 0 and j == 0), stop=(g == NG - 1 and j == 1)))
                sc.op("tensor", pvb,
                      reads=[f"PT{pti}", f"Va{wpar}_{2 * g}", f"Va{wpar}_{2 * g + 1}",
                             "Va_ones"],
                      writes=[f"po{h}"])

            sg2p = {}
            emit_S2(0)
            flush_norms()
            for g in range(NG):
                emit_exp(g, 0)
                emit_exp(g, 1)
                if g + 1 < NG:
                    emit_S2(g + 1)
                emit_PV(g, 0)
                emit_PV(g, 1)
                if nxt is not None:
                    next(nxt, None)
            for h in range(2):
                hb = 64 * h
                sc.op("scalar", [lambda e, h=h: e.activation(
                    out=lnscr[0:1, 0:TAU], in_=po[h][64:65, 0:TAU], func=AF.Ln)],
                    reads=[f"po{h}", "lnscr"], writes=["lnscr"])
                sc.op("scalar", [lambda e, h=h: e.activation(
                    out=rz[:, h, :], in_=lnscr[0:1, 0:TAU], func=AF.Exp, scale=-1.0)],
                    reads=["lnscr"], writes=[f"rz{h}"])
                sc.op("scalar", [lambda e, h=h: e.copy(
                    out=otmp[:, h, :], in_=po[h][0:64, 0:TAU])],
                    reads=[f"po{h}"], writes=[f"otmp{h}"])

                def mk_norm(h=h, hb=hb, qb=qb, hp=hp, wpar=wpar):
                    def go():
                        sc.op("tensor", [lambda e: e.matmul(out=po[h][64:128, 0:TAU],
                              lhsT=onesrowb[0:1, 0:64], rhs=rz[:, h, :],
                              start=True, stop=True)],
                              reads=[f"rz{h}", "consts"], writes=[f"po{h}"])
                        sc.op("vector", [lambda e: e.tensor_tensor(
                            out=onrm[:, h, :], in0=otmp[:, h, :],
                            in1=po[h][64:128, 0:TAU], op=AluOpType.mult)],
                            reads=[f"otmp{h}", f"po{h}"], writes=[f"otmp{h}"])
                        sc.op("vector", [lambda e: e.tensor_scalar(
                            out=O_sb[:, wpar, qb * TAU:(qb + 1) * TAU][hb:hb + 64, :],
                            in0=onrm[:, h, :],
                            scalar1=bqkv[hb:hb + 64, 3 * hp + 2:3 * hp + 3], scalar2=None,
                            op0=AluOpType.add)],
                            reads=[f"otmp{h}", "bqkv"], writes=[f"O{wpar}_{h}_{qb}"])
                    return go

                pending_norm.append(mk_norm())

        flush_norms()
        if nxt is not None:
            for _ in nxt:
                pass

        if hp % 2 == 1:
            for t in range(NT):
                for mc in range(FC):
                    opar = (t * FC + mc) % 2
                    sc.op("tensor", [
                        lambda e, t=t, mc=mc, hp=hp, opar=opar: e.matmul(
                            out=po[opar][0:128, 0:TAU],
                            lhsT=wout_sb[:, hp - 1, mc * 128:(mc + 1) * 128],
                            rhs=O_sb[:, 0, t * TAU:(t + 1) * TAU],
                            start=True, stop=False),
                        lambda e, t=t, mc=mc, hp=hp, opar=opar: e.matmul(
                            out=po[opar][0:128, 0:TAU],
                            lhsT=wout_sb[:, hp, mc * 128:(mc + 1) * 128],
                            rhs=O_sb[:, 1, t * TAU:(t + 1) * TAU],
                            start=False, stop=True)],
                        reads=[f"O0_0_{t}", f"O0_1_{t}", f"O1_0_{t}", f"O1_1_{t}", "wout"],
                        writes=[f"po{opar}"])
                    sc.op("vector", [lambda e, t=t, mc=mc, opar=opar: e.tensor_tensor(
                        out=x[:, mc, t * TAU:(t + 1) * TAU], in0=x[:, mc, t * TAU:(t + 1) * TAU],
                        in1=po[opar][0:128, 0:TAU], op=AluOpType.add)],
                        reads=[f"po{opar}", f"x{mc}_{t}"], writes=[f"x{mc}_{t}"])

    if DBG:
        dma("sync", "dbg", d_x1, x[:, :, :],
            reads=[f"x{f}_{t}" for f in range(FC) for t in range(NT)])
    ln("ln2")

    for tc in range(NTC):
        ri = tc % 4
        prt_ap = lambda ri=ri: pS[ri // 2][:, ri % 2, 0:NE]
        prt_res = f"bS{ri // 2}{ri % 2}"
        builders = []
        for kc in range(FC):
            builders.append(lambda e, kc=kc, tc=tc, prt_ap=prt_ap: e.matmul(
                out=prt_ap(), lhsT=xn[:, kc, tc * 128:(tc + 1) * 128],
                rhs=wr_sb[:, kc, :], start=(kc == 0), stop=(kc == FC - 1)))
        sc.op("tensor", builders,
              reads=[f"xn{kc}_{tc * 128 // TAU}" for kc in range(FC)] + ["wr"],
              writes=[prt_res])
        sc.op("vector", [lambda e, prt_ap=prt_ap: e.reduce_max(out=r_m[:, 1:2], in_=prt_ap(),
              axis=AX.X, negate=True)],
              reads=[prt_res], writes=["r_m1"])
        sc.op("scalar", [lambda e, prt_ap=prt_ap: e.activation(out=r_e[:], in_=prt_ap(), func=AF.Exp,
              bias=r_m[:, 1:2])], reads=[prt_res, "r_m1"], writes=["r_e"])
        sc.op("vector", [lambda e: e.reduce_max(out=r_m[:, 0:1], in_=r_e[:], axis=AX.X)],
              reads=["r_e"], writes=["r_m0"])
        sc.op("vector", [lambda e: e.scalar_tensor_tensor(out=r_t1[:], in0=r_e[:],
              scalar=r_m[:, 0:1], in1=r_e[:], op0=AluOpType.is_lt, op1=AluOpType.mult)],
              reads=["r_e", "r_m0"], writes=["r_t1"])
        sc.op("vector", [lambda e: e.reduce_max(out=r_m[:, 1:2], in_=r_t1[:], axis=AX.X)],
              reads=["r_t1"], writes=["r_m1"])
        sc.op("vector", [lambda e: e.scalar_tensor_tensor(out=r_t1[:], in0=r_e[:],
              scalar=r_m[:, 1:2], in1=r_e[:], op0=AluOpType.is_ge, op1=AluOpType.mult)],
              reads=["r_e", "r_m1"], writes=["r_t1"])
        sc.op("vector", [lambda e: e.reduce_sum(out=r_m[:, 2:3], in_=r_t1[:], axis=AX.X)],
              reads=["r_t1"], writes=["r_m2"])
        sc.op("vector", [lambda e: e.reciprocal(out=r_m[:, 3:4], in_=r_m[:, 2:3])],
              reads=["r_m2"], writes=["r_m3"])
        sc.op("vector", [lambda e, tc=tc: e.tensor_scalar(out=m_tmb[:, tc, :], in0=r_t1[:],
              scalar1=r_m[:, 3:4], scalar2=None, op0=AluOpType.mult)],
              reads=["r_t1", "r_m3"], writes=[f"m_tm{tc}"])
    NQ4 = (T + 511) // 512

    def build_mrep(e_idx, pmr_ap, pmr_res, pbm_ap, pbm_res):
        par = e_idx % 2
        for q4 in range(NQ4):
            n = min(512, T - q4 * 512)
            nck = n // 128
            builders = []
            for t4 in range(nck):
                tc = (q4 * 512) // 128 + t4
                builders.append(lambda e, tc=tc, t4=t4, e_idx=e_idx: e.matmul(
                    out=pmr_ap()[0:1, t4 * 128:(t4 + 1) * 128],
                    lhsT=m_tmb[:, tc, e_idx:e_idx + 1], rhs=eyeb[:],
                    start=True, stop=True))
            sc.op("tensor", builders,
                  reads=[f"m_tm{(q4 * 512) // 128 + t4}" for t4 in range(nck)] + ["consts_eye"],
                  writes=[pmr_res])
            sc.op("vector", [lambda e, n=n: e.tensor_copy(out=mrow[0:1, 0:n],
                  in_=pmr_ap()[0:1, 0:n])], reads=[pmr_res], writes=["mrow"])
            sc.op("tensor", [lambda e, n=n: e.matmul(out=pbm_ap()[:, 0:n], lhsT=onesrowb[:],
                  rhs=mrow[0:1, 0:n], start=True, stop=True)],
                  reads=["mrow", "consts"], writes=[pbm_res])
            sc.op("vector", [lambda e, par=par, q4=q4, n=n: e.tensor_copy(
                out=m_rep[:, par, q4 * 512:q4 * 512 + n], in_=pbm_ap()[:, 0:n])],
                reads=[pbm_res], writes=[f"m_rep{par}"])

    build_mrep(0, lambda: pS[2][0:1, 0, 0:512], "bS20",
               lambda: pS[2][:, 1, 0:512], "bS21")
    if DBG:
        dma("sync", "dbg", d_xn, xn[:, :, :],
            reads=[f"xn{f}_{t}" for f in range(FC) for t in range(NT)])
        dma("sync", "dbg", d_m, m_tm[:, :, :],
            reads=[f"m_tm{tc}" for tc in range(NTC)])
        sc.wait_raw("sync", "dbg", sc.cnt["dbg"])
    flush_phase()
    esB.close()

    esD = ExitStack()
    sbD = lambda name, shape, dt: esD.enter_context(nc.sbuf_tensor(name, shape, dt))
    wg_sb = sbD("wgs", [128, 2, FC, SLAB], BF16)
    wu_sb = sbD("wus", [128, 2, FC, SLAB], BF16)
    wd_sb = sbD("wds", [128, 2, NKD, H], BF16)
    tbuf = sbD("tbuf", [128, 3, NMC, TAU], BF16)
    sgscr = sbD("sgscr", [128, 2, TAU], F32)
    umscr = sbD("umscr", [128, 2, TAU], F32)
    pg = esD.enter_context(nc.psum_tensor("pg", [128, 2, 512], F32))
    pu = esD.enter_context(nc.psum_tensor("pu", [128, 2, 512], F32))
    py = esD.enter_context(nc.psum_tensor("py", [128, 3, 512], F32))
    pbm = esD.enter_context(nc.psum_tensor("pbm", [128, 512], F32))

    def load_slab(e_idx, s, par, split_first=False):
        if e_idx < NE:
            srcs = [dwg[e_idx, :, s * SLAB:(s + 1) * SLAB],
                    dwu[e_idx, :, s * SLAB:(s + 1) * SLAB],
                    dwd[e_idx, s * SLAB:(s + 1) * SLAB, :]]
        else:
            srcs = [dsg[:, s * SLAB:(s + 1) * SLAB],
                    dsu[:, s * SLAB:(s + 1) * SLAB],
                    dsd[s * SLAB:(s + 1) * SLAB, :]]
        if split_first:
            dma_group("gpsimd", "dwf2", [
                (wg_sb[:, par, :, :], srcs[0].rearrange("(kc p) d -> p kc d", p=128),
                 (), [f"wg{par}"])])
            dma_group("gpsimd", f"dwf{par}", [
                (wu_sb[:, par, :, :], srcs[1].rearrange("(kc p) d -> p kc d", p=128),
                 (), [f"wu{par}"]),
                (wd_sb[:, par, :, :], srcs[2].rearrange("(kc p) h -> p kc h", p=128),
                 (), [f"wd{par}"]),
            ])
            return
        dma_group("gpsimd", f"dwf{par}", [
            (wg_sb[:, par, :, :], srcs[0].rearrange("(kc p) d -> p kc d", p=128),
             (), [f"wg{par}"]),
            (wu_sb[:, par, :, :], srcs[1].rearrange("(kc p) d -> p kc d", p=128),
             (), [f"wu{par}"]),
            (wd_sb[:, par, :, :], srcs[2].rearrange("(kc p) h -> p kc h", p=128),
             (), [f"wd{par}"]),
        ])

    dyr = dy.rearrange("(fc p) t -> p fc t", p=128)

    def emit_wd(e_idx, s, t, par, tpar):
        for hc in range(FC):
            ppar = hc % 3
            builders = []
            for kd in range(NKD):
                builders.append(lambda e, kd=kd, hc=hc, par=par, tpar=tpar, ppar=ppar: e.matmul(
                    out=py[:, ppar, 0:TAU],
                    lhsT=wd_sb[:, par, kd, hc * 128:(hc + 1) * 128],
                    rhs=tbuf[:, tpar, kd, :],
                    start=(kd == 0), stop=(kd == NKD - 1)))
            sc.op("tensor", builders,
                  reads=[f"wd{par}"] + [f"tbuf{tpar}_{kd}" for kd in range(NKD)],
                  writes=[f"py{ppar}"])
            sc.op("vector", [lambda e, hc=hc, t=t, ppar=ppar: e.tensor_tensor(
                out=x[:, hc, t * TAU:(t + 1) * TAU], in0=x[:, hc, t * TAU:(t + 1) * TAU],
                in1=py[:, ppar, 0:TAU], op=AluOpType.add)],
                reads=[f"py{ppar}", f"x{hc}_{t}"], writes=[f"x{hc}_{t}"])
        if e_idx == NE - 1 and s == NSL - 1:
            if t == NT - 1:
                h = FC // 2
                dma("sync", "dy", dyr[:, 0:h, t * TAU:(t + 1) * TAU],
                    x[:, 0:h, t * TAU:(t + 1) * TAU],
                    reads=[f"x{f}_{t}" for f in range(h)])
                dma("scalar", "dy2", dyr[:, h:FC, t * TAU:(t + 1) * TAU],
                    x[:, h:FC, t * TAU:(t + 1) * TAU],
                    reads=[f"x{f}_{t}" for f in range(h, FC)])
            else:
                dma("sync", "dy", dyr[:, :, t * TAU:(t + 1) * TAU],
                    x[:, :, t * TAU:(t + 1) * TAU],
                    reads=[f"x{f}_{t}" for f in range(FC)])

    load_slab(NE, 0, 0, split_first=True)

    pending = None
    gti = [0]
    e_seq = [NE] + list(range(NE))
    for pos, e_idx in enumerate(e_seq):
        for s in range(NSL):
            par = (pos * NSL + s) % 2
            for t in range(NT):
                tpar = gti[0] % 3
                gti[0] += 1
                for mc in range(NMC):
                    gpar = mc % 2
                    gb = []
                    ub = []
                    for kc in range(FC):
                        gb.append(lambda e, kc=kc, mc=mc, t=t, par=par, gpar=gpar: e.matmul(
                            out=pg[:, gpar, 0:TAU],
                            lhsT=wg_sb[:, par, kc, mc * 128:(mc + 1) * 128],
                            rhs=xn[:, kc, t * TAU:(t + 1) * TAU],
                            start=(kc == 0), stop=(kc == FC - 1)))
                        ub.append(lambda e, kc=kc, mc=mc, t=t, par=par, gpar=gpar: e.matmul(
                            out=pu[:, gpar, 0:TAU],
                            lhsT=wu_sb[:, par, kc, mc * 128:(mc + 1) * 128],
                            rhs=xn[:, kc, t * TAU:(t + 1) * TAU],
                            start=(kc == 0), stop=(kc == FC - 1)))
                    xnr = [f"xn{kc}_{t}" for kc in range(FC)]
                    sc.op("tensor", gb, reads=[f"wg{par}"] + xnr, writes=[f"pg{gpar}"])
                    sc.op("tensor", ub, reads=[f"wu{par}"] + xnr, writes=[f"pu{gpar}"])
                    sc.op("scalar", [lambda e, gpar=gpar: e.activation(
                        out=sgscr[:, gpar, :], in_=pg[:, gpar, 0:TAU], func=AF.Silu)],
                        reads=[f"pg{gpar}"], writes=[f"sgscr{gpar}"])
                    if e_idx < NE:
                        sc.op("vector", [lambda e, gpar=gpar, e_idx=e_idx, t=t: e.tensor_tensor(
                            out=umscr[:, gpar, :], in0=pu[:, gpar, 0:TAU],
                            in1=m_rep[:, e_idx % 2, t * TAU:(t + 1) * TAU],
                            op=AluOpType.mult)],
                            reads=[f"pu{gpar}", f"m_rep{e_idx % 2}"],
                            writes=[f"umscr{gpar}"])
                        sc.op("vector", [lambda e, gpar=gpar, tpar=tpar, mc=mc: e.tensor_tensor(
                            out=tbuf[:, tpar, mc, :], in0=sgscr[:, gpar, :],
                            in1=umscr[:, gpar, :], op=AluOpType.mult)],
                            reads=[f"sgscr{gpar}", f"umscr{gpar}"],
                            writes=[f"tbuf{tpar}_{mc}"])
                    else:
                        sc.op("vector", [lambda e, gpar=gpar, tpar=tpar, mc=mc: e.tensor_tensor(
                            out=tbuf[:, tpar, mc, :], in0=sgscr[:, gpar, :],
                            in1=pu[:, gpar, 0:TAU], op=AluOpType.mult)],
                            reads=[f"sgscr{gpar}", f"pu{gpar}"],
                            writes=[f"tbuf{tpar}_{mc}"])
                if pending is not None:
                    emit_wd(*pending)
                pending = (e_idx, s, t, par, tpar)
                if t == 0:
                    nxt = pos * NSL + s + 1
                    if nxt < (NE + 1) * NSL:
                        load_slab(e_seq[nxt // NSL], nxt % NSL, nxt % 2)
                    if s == 0 and pos + 1 < len(e_seq) and 1 <= e_seq[pos + 1] < NE:
                        build_mrep(e_seq[pos + 1], lambda: py[0:1, 2, 0:512], "py2",
                                   lambda: pbm[:, :], "pbm")
    emit_wd(*pending)
    sc.wait_raw("sync", "dy", sc.cnt["dy"])
    sc.wait_raw("sync", "dy2", sc.cnt["dy2"])
    flush_phase()
    esD.close()
    es0.close()
    return nc


def _host_prep(L0_attn_X, L0_attn_W_qkv, L0_attn_b_qkv, L0_attn_W_out, L0_attn_b_out,
               L0_ffn_Wr, L0_ffn_Wg, L0_ffn_Wu, L0_ffn_Wd, L0_ffn_Sg, L0_ffn_Su, L0_ffn_Sd,
               n_cores=8):
    f32 = np.float32
    X = np.asarray(L0_attn_X, f32)
    Wqkv = np.asarray(L0_attn_W_qkv, f32)
    bqkv = np.asarray(L0_attn_b_qkv, f32)
    Wout = np.asarray(L0_attn_W_out, f32)
    bout = np.asarray(L0_attn_b_out, f32)
    Wr = np.asarray(L0_ffn_Wr, f32)
    Wg = np.asarray(L0_ffn_Wg, f32)
    Wu = np.asarray(L0_ffn_Wu, f32)
    Wd = np.asarray(L0_ffn_Wd, f32)
    Sg = np.asarray(L0_ffn_Sg, f32)
    Su = np.asarray(L0_ffn_Su, f32)
    Sd = np.asarray(L0_ffn_Sd, f32)

    perm = []
    for hp in range(8):
        h0, h1 = 2 * hp, 2 * hp + 1
        perm += list(range(h0 * 192, h0 * 192 + 64)) + list(range(h1 * 192, h1 * 192 + 64))
        perm += list(range(h0 * 192 + 64, h0 * 192 + 128)) + list(range(h1 * 192 + 64, h1 * 192 + 128))
        perm += list(range(h0 * 192 + 128, h0 * 192 + 192)) + list(range(h1 * 192 + 128, h1 * 192 + 192))
    perm = np.asarray(perm)
    wqkv_p = np.ascontiguousarray(Wqkv[:, perm])
    bqkv_p = np.ascontiguousarray(bqkv[perm].reshape(24, 128).T)
    bout_t = np.ascontiguousarray(bout.reshape(8, 128).T)
    wr_t = np.ascontiguousarray(Wr.reshape(8, 128, 8).transpose(1, 0, 2))

    common = dict(
        wqkv=wqkv_p, bqkv=bqkv_p, wout=np.ascontiguousarray(Wout), bout=bout_t,
        wr=wr_t, wg=np.ascontiguousarray(Wg), wu=np.ascontiguousarray(Wu),
        wd=np.ascontiguousarray(Wd), sg=np.ascontiguousarray(Sg),
        su=np.ascontiguousarray(Su), sd=np.ascontiguousarray(Sd),
        eye=np.eye(128, dtype=f32),
        eyeb=np.eye(128, dtype=ml_dtypes.bfloat16),
        onesrow=np.ones((1, 128), dtype=f32),
        onesrowb=np.ones((1, 128), dtype=ml_dtypes.bfloat16),
        onescol=np.ones((128, 1), dtype=ml_dtypes.bfloat16),
    )
    in_maps = []
    for c in range(n_cores):
        m = dict(common)
        m["x_in"] = np.ascontiguousarray(X[c].T)
        in_maps.append(m)
    return in_maps


def kernel(**inputs):
    in_maps = _host_prep(**inputs)
    cfg = dict(T=2048, TAU=512, DFF=2048, SLAB=512)
    nc = build_nc(cfg)
    res = run_bass_kernel_spmd(nc, in_maps, core_ids=list(range(8)))
    outs = [np.asarray(r["y"], dtype=np.float32).T for r in res.results]
    return np.stack(outs, axis=0)


# version 41
# speedup vs baseline: 1.0016x; 1.0016x over previous
import numpy as np
import ml_dtypes
import concourse.bass as bass
import concourse.mybir as mybir
from concourse.bass_utils import run_bass_kernel_spmd
from concourse.alu_op_type import AluOpType
from contextlib import ExitStack

F32 = mybir.dt.float32
BF16 = mybir.dt.bfloat16
AF = mybir.ActivationFunctionType
AX = mybir.AxisListType

H = 1024
FC = H // 128
NPAIR = 8
NE = 8
LN_EPS = 1e-5

ENG_SEM = {"tensor": "tp", "vector": "tv", "scalar": "ta", "gpsimd": "tg", "sync": "ts"}
SELF_SYNC_ENGS = ("vector", "scalar", "gpsimd")
ALL_ENGS = ("tensor", "vector", "scalar", "gpsimd", "sync")


class Sched:

    def __init__(self, semnames):
        self.q = {e: [] for e in ALL_ENGS}
        self.cnt = {s: 0 for s in semnames}
        self.known = {e: {} for e in ALL_ENGS}
        self.last_implies = {e: {} for e in ALL_ENGS}
        self.writer = {}
        self.readers = {}
        self.cuts = []

    def _need(self, eng, dep):
        deng, sem, val, implies = dep
        if deng == eng and eng not in SELF_SYNC_ENGS:
            return
        k = self.known[eng]
        if k.get(sem, 0) >= val:
            return
        self.q[eng].append(("wait", sem, val))
        for s2, v2 in implies.items():
            if k.get(s2, 0) < v2:
                k[s2] = v2
        if k.get(sem, 0) < val:
            k[sem] = val

    def op(self, eng, builders, reads=(), writes=(), sem=None, amt=1):
        sem = sem or ENG_SEM[eng]
        for r in reads:
            d = self.writer.get(r)
            if d is not None:
                self._need(eng, d)
        for w in writes:
            d = self.writer.get(w)
            if d is not None:
                self._need(eng, d)
            for dep in list(self.readers.get(w, {}).values()):
                self._need(eng, dep)
        self.q[eng].append(("ops", builders, sem, amt))
        self.cnt[sem] += amt
        val = self.cnt[sem]
        if sem not in ENG_SEM.values():
            implies = dict(self.last_implies[eng])
            implies.update(self.known[eng])
            implies[sem] = val
            dep = ("dma", sem, val, implies)
            for r in reads:
                self.readers.setdefault(r, {})[sem] = dep
            for w in writes:
                self.writer[w] = dep
                self.readers[w] = {}
            return dep
        implies = dict(self.last_implies[eng])
        implies.update(self.known[eng])
        implies[sem] = val
        self.last_implies[eng] = implies
        dep = (eng, sem, val, implies)
        for r in reads:
            self.readers.setdefault(r, {})[sem] = dep
        for w in writes:
            self.writer[w] = dep
            self.readers[w] = {}
        return dep

    def wait_raw(self, eng, sem, val):
        if self.known[eng].get(sem, 0) < val:
            self.q[eng].append(("wait", sem, val))
            self.known[eng][sem] = val

    def cut(self):
        self.cuts.append({e: len(self.q[e]) for e in ALL_ENGS})
        done = {s: self.cnt[s] for s in self.cnt if s not in ("dxa", "dx0", "dy")}
        for e in ALL_ENGS:
            k = self.known[e]
            for s, v in done.items():
                if k.get(s, 0) < v:
                    k[s] = v

    def replay(self, nc, sems, seg, block):
        starts = self.cuts[seg - 1] if seg > 0 else {e: 0 for e in ALL_ENGS}
        ends = self.cuts[seg] if seg < len(self.cuts) else {e: len(self.q[e]) for e in ALL_ENGS}

        def run(engname, handle):
            for item in self.q[engname][starts[engname]:ends[engname]]:
                if item[0] == "wait":
                    handle.wait_ge(sems[item[1]], item[2])
                else:
                    _, builders, sem, amt = item
                    ins = None
                    for b in builders:
                        ins = b(handle)
                    ins.then_inc(sems[sem], amt)

        @block.tensor
        def _(t):
            run("tensor", t)

        @block.vector
        def _(v):
            run("vector", v)

        @block.scalar
        def _(s):
            run("scalar", s)

        @block.gpsimd
        def _(g):
            run("gpsimd", g)

        @block.sync
        def _(sy):
            run("sync", sy)


def build_nc(cfg):
    T = cfg["T"]; TAU = cfg["TAU"]; DFF = cfg["DFF"]; SLAB = cfg["SLAB"]
    NT = T // TAU
    NTC = T // 128
    NG = NTC // 2
    NSL = DFF // SLAB
    NMC = SLAB // 128
    NKD = SLAB // 128
    assert TAU <= 512 and T % TAU == 0 and T % 256 == 0 and SLAB % 128 == 0

    nc = bass.Bass("TRN2", target_bir_lowering=False)

    dx_in = nc.dram_tensor("x_in", [H, T], F32, kind="ExternalInput").ap()
    dwqkv = nc.dram_tensor("wqkv", [H, 3 * H], F32, kind="ExternalInput").ap()
    dbqkv = nc.dram_tensor("bqkv", [128, 24], F32, kind="ExternalInput").ap()
    dwout = nc.dram_tensor("wout", [H, H], F32, kind="ExternalInput").ap()
    dbout = nc.dram_tensor("bout", [128, FC], F32, kind="ExternalInput").ap()
    dwr = nc.dram_tensor("wr", [128, FC, NE], F32, kind="ExternalInput").ap()
    dwg = nc.dram_tensor("wg", [NE, H, DFF], F32, kind="ExternalInput").ap()
    dwu = nc.dram_tensor("wu", [NE, H, DFF], F32, kind="ExternalInput").ap()
    dwd = nc.dram_tensor("wd", [NE, DFF, H], F32, kind="ExternalInput").ap()
    dsg = nc.dram_tensor("sg", [H, DFF], F32, kind="ExternalInput").ap()
    dsu = nc.dram_tensor("su", [H, DFF], F32, kind="ExternalInput").ap()
    dsd = nc.dram_tensor("sd", [DFF, H], F32, kind="ExternalInput").ap()
    deye = nc.dram_tensor("eye", [128, 128], F32, kind="ExternalInput").ap()
    deyeb = nc.dram_tensor("eyeb", [128, 128], BF16, kind="ExternalInput").ap()
    d1row = nc.dram_tensor("onesrow", [1, 128], F32, kind="ExternalInput").ap()
    d1col = nc.dram_tensor("onescol", [128, 1], BF16, kind="ExternalInput").ap()
    d1rowb = nc.dram_tensor("onesrowb", [1, 128], BF16, kind="ExternalInput").ap()
    dy = nc.dram_tensor("y", [H, T], F32, kind="ExternalOutput").ap()
    DBG = cfg.get("dbg", False)
    if DBG:
        d_x1 = nc.dram_tensor("dbg_x1", [128, FC, T], F32, kind="ExternalOutput").ap()
        d_xn = nc.dram_tensor("dbg_xn", [128, FC, T], BF16, kind="ExternalOutput").ap()
        d_m = nc.dram_tensor("dbg_m", [128, T // 128, NE], F32, kind="ExternalOutput").ap()

    semnames = ["tp", "tv", "ta", "tg", "ts", "dxa", "dx0", "dy",
                "dwq0", "dwq1", "dwo", "dwf0", "dwf1", "dwf2", "dbg"]
    sc = Sched(semnames)

    es0 = ExitStack()
    sb = lambda name, shape, dt: es0.enter_context(nc.sbuf_tensor(name, shape, dt))
    x = sb("x", [128, FC, T], F32)
    xn = sb("xn", [128, FC, T], BF16)
    eye = sb("eyes", [128, 128], F32)
    eyeb = sb("eyebs", [128, 128], BF16)
    m_tmb = sb("m_tmb", [128, NTC, NE], BF16)
    onesrow = sb("onesrows", [1, 128], F32)
    onescol = sb("onescols", [128, 1], BF16)
    onesrowb = sb("onesrowbs", [1, 128], BF16)
    bqkv = sb("bqkvs", [128, 24], F32)
    bqs = sb("bqss", [128, 8], F32)
    bout = sb("bouts", [128, FC], F32)
    wr_sb = sb("wrs", [128, FC, NE], BF16)
    xb = sb("xb", [128, 2, TAU], BF16)
    xsq = sb("xsq", [128, 2, TAU], BF16)
    s_mu = sb("s_mu", [1, TAU], F32)
    s_ex2 = sb("s_ex2", [1, TAU], F32)
    s_var = sb("s_var", [1, TAU], F32)
    s_rstd = sb("s_rstd", [1, TAU], F32)
    s_std = s_ex2
    s_bln = s_mu
    lnscr = sb("lnscr", [128, TAU], F32)
    epsc = sb("epsc", [1, 1], F32)
    m_rep = sb("m_rep", [128, 2, T], F32)
    r_e = sb("r_e", [128, NE], F32)
    r_t1 = sb("r_t1", [128, NE], F32)
    r_m = sb("r_m", [128, 4], F32)
    mrow = sb("mrow", [1, 512], BF16)

    sems = {n: es0.enter_context(nc.semaphore(n)) for n in semnames}
    seg_idx = [0]

    def flush_phase():
        sc.cut()
        with nc.Block() as block:
            sc.replay(nc, sems, seg_idx[0], block)
        seg_idx[0] += 1

    def dma_group(queue_eng, dmasem, items):
        allw = []
        dep = None
        for dst, src, reads, writes in items:
            dep = sc.op(queue_eng, [lambda e, d=dst, s=src: e.dma_start(out=d, in_=s)],
                        reads=reads, writes=writes, sem=dmasem, amt=16)
            allw += list(writes)
        for r in allw:
            sc.writer[r] = dep
        return dep

    def dma(queue_eng, dmasem, dst, src, reads=(), writes=()):
        return dma_group(queue_eng, dmasem, [(dst, src, reads, writes)])

    def ln(which):
        inv_h = 1.0 / H

        def sums(t):
            tp_ = t % 2
            xres = [f"x{f}_{t}" for f in range(FC)]
            for f in range(FC):
                par = f % 2
                cast_eng = "gpsimd" if f < 4 else "scalar"
                if cast_eng == "gpsimd":
                    sc.op("gpsimd", [lambda e, f=f, t=t, par=par: e.tensor_copy(
                        out=xb[:, par, :], in_=x[:, f, t * TAU:(t + 1) * TAU])],
                        reads=[xres[f]], writes=[f"xb{par}"])
                else:
                    sc.op("scalar", [lambda e, f=f, t=t, par=par: e.copy(
                        out=xb[:, par, :], in_=x[:, f, t * TAU:(t + 1) * TAU])],
                        reads=[xres[f]], writes=[f"xb{par}"])
                sc.op("scalar", [lambda e, f=f, t=t, par=par: e.activation(
                    out=xsq[:, par, :], in_=x[:, f, t * TAU:(t + 1) * TAU], func=AF.Square)],
                    reads=[xres[f]], writes=[f"xsq{par}"])
                sc.op("tensor", [lambda e, f=f, par=par, tp_=tp_: e.matmul(
                    out=pS[tp_][0:1, 0, 0:TAU], lhsT=onescol[:], rhs=xb[:, par, :],
                    start=(f == 0), stop=(f == FC - 1))],
                    reads=[f"xb{par}", "consts"], writes=[f"bS{tp_}0"])
                sc.op("tensor", [lambda e, f=f, par=par, tp_=tp_: e.matmul(
                    out=pS[tp_][0:1, 1, 0:TAU], lhsT=onescol[:], rhs=xsq[:, par, :],
                    start=(f == 0), stop=(f == FC - 1))],
                    reads=[f"xsq{par}", "consts"], writes=[f"bS{tp_}1"])

        def rest(t):
            tp_ = t % 2
            s1r, s2r = f"bS{tp_}0", f"bS{tp_}1"
            xres = [f"x{f}_{t}" for f in range(FC)]
            sc.op("vector", [lambda e, tp_=tp_: e.tensor_scalar(out=s_mu[:],
                  in0=pS[tp_][0:1, 0, 0:TAU],
                  scalar1=inv_h, scalar2=None, op0=AluOpType.mult)],
                  reads=[s1r], writes=["s_mu"])
            sc.op("vector", [lambda e, tp_=tp_: e.tensor_scalar(out=s_ex2[:],
                  in0=pS[tp_][0:1, 1, 0:TAU],
                  scalar1=inv_h, scalar2=None, op0=AluOpType.mult)],
                  reads=[s2r], writes=["s_ex2"])
            sc.op("vector", [lambda e: e.tensor_tensor(out=s_var[:], in0=s_mu[:], in1=s_mu[:],
                  op=AluOpType.mult)], reads=["s_mu"], writes=["s_var"])
            sc.op("vector", [lambda e: e.tensor_tensor(out=s_var[:], in0=s_ex2[:], in1=s_var[:],
                  op=AluOpType.subtract)], reads=["s_ex2", "s_var"], writes=["s_var"])
            sc.op("scalar", [lambda e: e.activation(out=s_std[:], in_=s_var[:], func=AF.Sqrt,
                  bias=epsc[0:1, 0:1])], reads=["s_var", "epsc", "s_ex2"], writes=["s_ex2"])
            sc.op("vector", [lambda e: e.reciprocal(out=s_rstd[:], in_=s_std[:])],
                  reads=["s_ex2"], writes=["s_rstd"])
            sc.op("vector", [lambda e: e.scalar_tensor_tensor(out=s_bln[:], in0=s_mu[:],
                  scalar=-1.0, in1=s_rstd[:], op0=AluOpType.mult, op1=AluOpType.mult)],
                  reads=["s_mu", "s_rstd"], writes=["s_mu"])
            sc.op("tensor", [lambda e: e.matmul(out=po[0][:, 0:TAU], lhsT=onesrow[:],
                  rhs=s_rstd[:], start=True, stop=True)],
                  reads=["s_rstd", "consts"], writes=["po0"])
            sc.op("tensor", [lambda e: e.matmul(out=po[1][:, 0:TAU], lhsT=onesrow[:],
                  rhs=s_bln[:], start=True, stop=True)],
                  reads=["s_mu", "consts"], writes=["po1"])
            sc.op("vector", [lambda e: e.tensor_copy(out=a_sb[:], in_=po[0][:, 0:TAU])],
                  reads=["po0"], writes=["a_sb"])
            sc.op("vector", [lambda e: e.tensor_copy(out=b_sb[:], in_=po[1][:, 0:TAU])],
                  reads=["po1"], writes=["b_sb"])
            for f in range(FC):
                if f < 5:
                    sc.op("vector", [lambda e, f=f, t=t: e.tensor_tensor(
                        out=lnscr[:], in0=x[:, f, t * TAU:(t + 1) * TAU], in1=po[0][:, 0:TAU],
                        op=AluOpType.mult)], reads=[xres[f], "po0"], writes=["lnscr"])
                    sc.op("vector", [lambda e, f=f, t=t: e.tensor_tensor(
                        out=xn[:, f, t * TAU:(t + 1) * TAU], in0=lnscr[:], in1=po[1][:, 0:TAU],
                        op=AluOpType.add)], reads=["lnscr", "po1"], writes=[f"xn{f}_{t}"])
                else:
                    sc.op("gpsimd", [lambda e, f=f, t=t: e.tensor_tensor(
                        out=xn[:, f, t * TAU:(t + 1) * TAU],
                        in0=x[:, f, t * TAU:(t + 1) * TAU], in1=a_sb[:],
                        op=AluOpType.mult)],
                        reads=[xres[f], "a_sb"], writes=[f"xn{f}_{t}"])
                    sc.op("gpsimd", [lambda e, f=f, t=t: e.tensor_tensor(
                        out=xn[:, f, t * TAU:(t + 1) * TAU],
                        in0=xn[:, f, t * TAU:(t + 1) * TAU], in1=b_sb[:],
                        op=AluOpType.add)],
                        reads=[f"xn{f}_{t}", "b_sb"], writes=[f"xn{f}_{t}"])

        sums(0)
        for t in range(NT):
            if t + 1 < NT:
                sums(t + 1)
            rest(t)

    esB = ExitStack()
    sbB = lambda name, shape, dt: esB.enter_context(nc.sbuf_tensor(name, shape, dt))
    wq_sb = sbB("wq", [128, 2, FC, 384], BF16)
    wout_sb = sbB("wouts", [128, FC, H], BF16)
    Q = sbB("Qs", [128, 2, T], BF16)
    Kt = sbB("Kts", [128, 2, T], BF16)
    Va = sbB("Va", [128, 2, 2, NTC, 65], BF16)
    PT = sbB("PTs", [128, 6, 2 * TAU], BF16)
    O_sb = sbB("Os", [128, 2, T], BF16)
    otmp = sbB("otmp", [64, 2, TAU], BF16)
    onrm = otmp
    rz = sbB("rz", [1, 2, TAU], BF16)
    kb = sbB("kb", [128, 4], F32)
    a_sb = sbB("a_sb", [128, TAU], BF16)
    b_sb = sbB("b_sb", [128, TAU], BF16)

    pS = [esB.enter_context(nc.psum_tensor(f"pS{i}", [128, 2, 512], F32))
          for i in range(3)]
    po0 = esB.enter_context(nc.psum_tensor("po0", [128, 512], F32))
    po1 = esB.enter_context(nc.psum_tensor("po1", [128, 512], F32))
    po = [po0, po1]

    dxr = dx_in.rearrange("(fc p) t -> p fc t", p=128)
    dma_group("sync", "dx0", [
        (x[:, :, 0:TAU], dxr[:, :, 0:TAU], (), [f"x{f}_0" for f in range(FC)]),
        (eye[:], deye, (), ["consts_eye"]),
        (eyeb[:], deyeb, (), ["consts_eye"]),
        (onesrow[:], d1row, (), ["consts"]),
        (onescol[:], d1col, (), ["consts"]),
        (onesrowb[:], d1rowb, (), ["consts"]),
        (bqkv[:], dbqkv, (), ["bqkv"]),
        (bout[:], dbout, (), ["bout"]),
    ])
    if NT > 1:
        dma_group("sync", "dxa", [
            (x[:, :, TAU:T], dxr[:, :, TAU:T], (),
             [f"x{f}_{t}" for f in range(FC) for t in range(1, NT)])])
    dma("gpsimd", "dwo", wr_sb[:], dwr, writes=["wr"])
    sc.op("gpsimd", [lambda e: e.memset(epsc[:], LN_EPS)], writes=["epsc"])
    sc.op("vector", [lambda e: e.tensor_scalar(out=bqs[:], in0=bqkv[:, 0:24:3],
          scalar1=0.125, scalar2=None, op0=AluOpType.mult)],
          reads=["bqkv"], writes=["bqs"])

    ln("ln1")


    for f in range(FC):
        sc.op("vector", [lambda e, f=f: e.tensor_scalar(out=x[:, f, :], in0=x[:, f, :],
              scalar1=bout[:, f:f + 1], scalar2=None, op0=AluOpType.add)],
              reads=["bout"] + [f"x{f}_{t}" for t in range(NT)],
              writes=[f"x{f}_{t}" for t in range(NT)])

    sc.op("gpsimd", [lambda e: e.memset(Va[:, 0, 0, :, 64:65], 1.0),
                     lambda e: e.memset(Va[:, 0, 1, :, 64:65], 1.0),
                     lambda e: e.memset(Va[:, 1, 0, :, 64:65], 1.0),
                     lambda e: e.memset(Va[:, 1, 1, :, 64:65], 1.0)],
          writes=["Va_ones"])

    dma("gpsimd", "dwq0", wq_sb[:, 0, :, :],
        dwqkv[:, 0:384].rearrange("(kc p) c -> p kc c", p=128), writes=["wq0"])

    pool_i = [0]

    def take_pair():
        i = pool_i[0]
        pool_i[0] = (i + 1) % 3
        return i

    def gen_qkv(hp):
        wpar = hp % 2
        for t in range(NT):
            for part, coloff in (("q", 0), ("k", 128)):
                pi = take_pair()
                bank = 0
                pt = pS[pi]
                res = f"bS{pi}0"
                builders = []
                for kc in range(FC):
                    builders.append(lambda e, kc=kc, wpar=wpar, coloff=coloff, t=t,
                                    bank=bank, pt=pt: e.matmul(
                        out=pt[:, bank, 0:TAU],
                        lhsT=wq_sb[:, wpar, kc, coloff:coloff + 128],
                        rhs=xn[:, kc, t * TAU:(t + 1) * TAU],
                        start=(kc == 0), stop=(kc == FC - 1)))
                sc.op("tensor", builders,
                      reads=[f"xn{kc}_{t}" for kc in range(FC)] + [f"wq{wpar}"],
                      writes=[res])
                if part == "q":
                    sc.op("vector", [lambda e, t=t, pt=pt, hp=hp, wpar=wpar, bank=bank:
                          e.tensor_scalar(
                        out=Q[:, wpar, t * TAU:(t + 1) * TAU], in0=pt[:, bank, 0:TAU],
                        scalar1=0.125, scalar2=bqs[:, hp:hp + 1],
                        op0=AluOpType.mult, op1=AluOpType.add)],
                        reads=[res, "bqs"], writes=[f"Q{wpar}_{t}"])
                else:
                    sc.op("vector", [lambda e, t=t, pt=pt, hp=hp, wpar=wpar, bank=bank:
                          e.tensor_scalar(
                        out=Kt[:, wpar, t * TAU:(t + 1) * TAU], in0=pt[:, bank, 0:TAU],
                        scalar1=bqkv[:, 3 * hp + 1:3 * hp + 2], scalar2=None,
                        op0=AluOpType.add)],
                        reads=[res, "bqkv"], writes=[f"K{wpar}_{t}"])
                yield
        for tc in range(NTC):
            pi = take_pair()
            bank = 0
            pt = pS[pi]
            res = f"bS{pi}0"
            builders = []
            for kc in range(FC):
                builders.append(lambda e, kc=kc, tc=tc, wpar=wpar, bank=bank, pt=pt:
                                e.matmul(
                    out=pt[:, bank, 0:128], lhsT=xn[:, kc, tc * 128:(tc + 1) * 128],
                    rhs=wq_sb[:, wpar, kc, 256:384],
                    start=(kc == 0), stop=(kc == FC - 1)))
            sc.op("tensor", builders,
                  reads=[f"xn{kc}_{tc * 128 // TAU}" for kc in range(FC)] + [f"wq{wpar}"],
                  writes=[res])
            sc.op("vector", [lambda e, tc=tc, pt=pt, wpar=wpar, bank=bank: e.tensor_copy(
                out=Va[:, wpar, 0, tc, 0:64], in_=pt[:, bank, 0:64])],
                reads=[res], writes=[f"Va{wpar}_{tc}"])
            sc.op("vector", [lambda e, tc=tc, pt=pt, wpar=wpar, bank=bank: e.tensor_copy(
                out=Va[:, wpar, 1, tc, 0:64], in_=pt[:, bank, 64:128])],
                reads=[res], writes=[f"Va{wpar}_{tc}"])
            yield
        kreads = [f"K{wpar}_{t}" for t in range(NT)]
        sc.op("vector", [lambda e, wpar=wpar: e.reduce_sum(
            out=kb[:, 2 * wpar:2 * wpar + 1], in_=Kt[:, wpar, :], axis=AX.X)],
            reads=kreads, writes=[f"kb{wpar}"])
        sc.op("vector", [lambda e, wpar=wpar: e.tensor_scalar(
            out=kb[:, 2 * wpar + 1:2 * wpar + 2], in0=kb[:, 2 * wpar:2 * wpar + 1],
            scalar1=1.0 / T, scalar2=None, op0=AluOpType.mult)],
            reads=[f"kb{wpar}"], writes=[f"kbs{wpar}"])
        sc.op("vector", [lambda e, wpar=wpar: e.tensor_scalar(
            out=Kt[:, wpar, :], in0=Kt[:, wpar, :],
            scalar1=kb[:, 2 * wpar + 1:2 * wpar + 2], scalar2=None,
            op0=AluOpType.subtract)],
            reads=[f"kbs{wpar}"] + kreads, writes=kreads)

    for _ in gen_qkv(0):
        pass

    for hp in range(NPAIR):
        wpar = hp % 2
        if hp + 1 < NPAIR:
            dma("gpsimd", f"dwq{(hp + 1) % 2}", wq_sb[:, (hp + 1) % 2, :, :],
                dwqkv[:, (hp + 1) * 384:(hp + 2) * 384].rearrange("(kc p) c -> p kc c", p=128),
                writes=[f"wq{(hp + 1) % 2}"])
            nxt = gen_qkv(hp + 1)
        else:
            nxt = None
        if hp == 0:
            dma("gpsimd", "dwo", wout_sb[:],
                dwout.rearrange("(rc p) c -> p rc c", p=128), writes=["wout"])

        pending_norm = []

        def flush_norms():
            for fn in pending_norm:
                fn()
            pending_norm.clear()

        for qb in range(NT):
            def emit_S2(g):
                pa = take_pair()
                pb = take_pair()
                sg2p[g] = (pa, pb)
                builders = []
                sres = []
                for h, pi in ((0, pa), (1, pb)):
                    hb = 64 * h
                    for j in range(2):
                        kc = 2 * g + j
                        builders.append(lambda e, kc=kc, j=j, hb=hb, qb=qb, pi=pi,
                                        wpar=wpar: e.matmul(
                            out=pS[pi][:, j, 0:TAU],
                            lhsT=Kt[:, wpar, kc * 128:(kc + 1) * 128][hb:hb + 64, :],
                            rhs=Q[:, wpar, qb * TAU:(qb + 1) * TAU][hb:hb + 64, :],
                            start=True, stop=True))
                    sres += [f"bS{pi}0", f"bS{pi}1"]
                kcr = [f"K{wpar}_{(2 * g) * 128 // TAU}",
                       f"K{wpar}_{(2 * g + 1) * 128 // TAU}", f"Q{wpar}_{qb}"]
                sc.op("tensor", builders, reads=kcr, writes=sres)

            def emit_exp(g, h):
                pi = sg2p[g][h]
                pti = (g % 3) * 2 + h
                sc.op("scalar",
                      [lambda e, pi=pi, pti=pti: e.activation(
                          out=PT[:, pti, :], in_=pS[pi][:, :, 0:TAU], func=AF.Exp)],
                      reads=[f"bS{pi}0", f"bS{pi}1"], writes=[f"PT{pti}"])

            def emit_PV(g, h):
                pti = (g % 3) * 2 + h
                pvb = []
                for j in range(2):
                    kc = 2 * g + j
                    pvb.append(lambda e, kc=kc, j=j, h=h, g=g, pti=pti,
                               wpar=wpar: e.matmul(
                        out=po[h][0:65, 0:TAU], lhsT=Va[:, wpar, h, kc, :],
                        rhs=PT[:, pti, j * TAU:(j + 1) * TAU],
                        start=(g == 0 and j == 0), stop=(g == NG - 1 and j == 1)))
                sc.op("tensor", pvb,
                      reads=[f"PT{pti}", f"Va{wpar}_{2 * g}", f"Va{wpar}_{2 * g + 1}",
                             "Va_ones"],
                      writes=[f"po{h}"])

            sg2p = {}
            emit_S2(0)
            flush_norms()
            for g in range(NG):
                emit_exp(g, 0)
                emit_exp(g, 1)
                if g + 1 < NG:
                    emit_S2(g + 1)
                emit_PV(g, 0)
                emit_PV(g, 1)
                if nxt is not None:
                    next(nxt, None)
            for h in range(2):
                hb = 64 * h
                sc.op("scalar", [lambda e, h=h: e.activation(
                    out=lnscr[0:1, 0:TAU], in_=po[h][64:65, 0:TAU], func=AF.Ln)],
                    reads=[f"po{h}", "lnscr"], writes=["lnscr"])
                sc.op("scalar", [lambda e, h=h: e.activation(
                    out=rz[:, h, :], in_=lnscr[0:1, 0:TAU], func=AF.Exp, scale=-1.0)],
                    reads=["lnscr"], writes=[f"rz{h}"])
                sc.op("scalar", [lambda e, h=h: e.copy(
                    out=otmp[:, h, :], in_=po[h][0:64, 0:TAU])],
                    reads=[f"po{h}"], writes=[f"otmp{h}"])

                def mk_norm(h=h, hb=hb, qb=qb, hp=hp, wpar=wpar):
                    def go():
                        sc.op("tensor", [lambda e: e.matmul(out=po[h][64:128, 0:TAU],
                              lhsT=onesrowb[0:1, 0:64], rhs=rz[:, h, :],
                              start=True, stop=True)],
                              reads=[f"rz{h}", "consts"], writes=[f"po{h}"])
                        sc.op("vector", [lambda e: e.tensor_tensor(
                            out=onrm[:, h, :], in0=otmp[:, h, :],
                            in1=po[h][64:128, 0:TAU], op=AluOpType.mult)],
                            reads=[f"otmp{h}", f"po{h}"], writes=[f"otmp{h}"])
                        sc.op("vector", [lambda e: e.tensor_scalar(
                            out=O_sb[:, wpar, qb * TAU:(qb + 1) * TAU][hb:hb + 64, :],
                            in0=onrm[:, h, :],
                            scalar1=bqkv[hb:hb + 64, 3 * hp + 2:3 * hp + 3], scalar2=None,
                            op0=AluOpType.add)],
                            reads=[f"otmp{h}", "bqkv"], writes=[f"O{wpar}_{h}_{qb}"])
                    return go

                pending_norm.append(mk_norm())

        flush_norms()
        if nxt is not None:
            for _ in nxt:
                pass

        if hp % 2 == 1:
            for t in range(NT):
                for mc in range(FC):
                    opar = (t * FC + mc) % 2
                    sc.op("tensor", [
                        lambda e, t=t, mc=mc, hp=hp, opar=opar: e.matmul(
                            out=po[opar][0:128, 0:TAU],
                            lhsT=wout_sb[:, hp - 1, mc * 128:(mc + 1) * 128],
                            rhs=O_sb[:, 0, t * TAU:(t + 1) * TAU],
                            start=True, stop=False),
                        lambda e, t=t, mc=mc, hp=hp, opar=opar: e.matmul(
                            out=po[opar][0:128, 0:TAU],
                            lhsT=wout_sb[:, hp, mc * 128:(mc + 1) * 128],
                            rhs=O_sb[:, 1, t * TAU:(t + 1) * TAU],
                            start=False, stop=True)],
                        reads=[f"O0_0_{t}", f"O0_1_{t}", f"O1_0_{t}", f"O1_1_{t}", "wout"],
                        writes=[f"po{opar}"])
                    sc.op("vector", [lambda e, t=t, mc=mc, opar=opar: e.tensor_tensor(
                        out=x[:, mc, t * TAU:(t + 1) * TAU], in0=x[:, mc, t * TAU:(t + 1) * TAU],
                        in1=po[opar][0:128, 0:TAU], op=AluOpType.add)],
                        reads=[f"po{opar}", f"x{mc}_{t}"], writes=[f"x{mc}_{t}"])

    if DBG:
        dma("sync", "dbg", d_x1, x[:, :, :],
            reads=[f"x{f}_{t}" for f in range(FC) for t in range(NT)])
    ln("ln2")

    for tc in range(NTC):
        ri = tc % 4
        prt_ap = lambda ri=ri: pS[ri // 2][:, ri % 2, 0:NE]
        prt_res = f"bS{ri // 2}{ri % 2}"
        builders = []
        for kc in range(FC):
            builders.append(lambda e, kc=kc, tc=tc, prt_ap=prt_ap: e.matmul(
                out=prt_ap(), lhsT=xn[:, kc, tc * 128:(tc + 1) * 128],
                rhs=wr_sb[:, kc, :], start=(kc == 0), stop=(kc == FC - 1)))
        sc.op("tensor", builders,
              reads=[f"xn{kc}_{tc * 128 // TAU}" for kc in range(FC)] + ["wr"],
              writes=[prt_res])
        sc.op("vector", [lambda e, prt_ap=prt_ap: e.reduce_max(out=r_m[:, 1:2], in_=prt_ap(),
              axis=AX.X, negate=True)],
              reads=[prt_res], writes=["r_m1"])
        sc.op("scalar", [lambda e, prt_ap=prt_ap: e.activation(out=r_e[:], in_=prt_ap(), func=AF.Exp,
              bias=r_m[:, 1:2])], reads=[prt_res, "r_m1"], writes=["r_e"])
        sc.op("vector", [lambda e: e.reduce_max(out=r_m[:, 0:1], in_=r_e[:], axis=AX.X)],
              reads=["r_e"], writes=["r_m0"])
        sc.op("vector", [lambda e: e.scalar_tensor_tensor(out=r_t1[:], in0=r_e[:],
              scalar=r_m[:, 0:1], in1=r_e[:], op0=AluOpType.is_lt, op1=AluOpType.mult)],
              reads=["r_e", "r_m0"], writes=["r_t1"])
        sc.op("vector", [lambda e: e.reduce_max(out=r_m[:, 1:2], in_=r_t1[:], axis=AX.X)],
              reads=["r_t1"], writes=["r_m1"])
        sc.op("vector", [lambda e: e.scalar_tensor_tensor(out=r_t1[:], in0=r_e[:],
              scalar=r_m[:, 1:2], in1=r_e[:], op0=AluOpType.is_ge, op1=AluOpType.mult)],
              reads=["r_e", "r_m1"], writes=["r_t1"])
        sc.op("vector", [lambda e: e.reduce_sum(out=r_m[:, 2:3], in_=r_t1[:], axis=AX.X)],
              reads=["r_t1"], writes=["r_m2"])
        sc.op("vector", [lambda e: e.reciprocal(out=r_m[:, 3:4], in_=r_m[:, 2:3])],
              reads=["r_m2"], writes=["r_m3"])
        sc.op("vector", [lambda e, tc=tc: e.tensor_scalar(out=m_tmb[:, tc, :], in0=r_t1[:],
              scalar1=r_m[:, 3:4], scalar2=None, op0=AluOpType.mult)],
              reads=["r_t1", "r_m3"], writes=[f"m_tm{tc}"])
    NQ4 = (T + 511) // 512

    def build_mrep(e_idx, pmr_ap, pmr_res, pbm_ap, pbm_res):
        par = e_idx % 2
        for q4 in range(NQ4):
            n = min(512, T - q4 * 512)
            nck = n // 128
            builders = []
            for t4 in range(nck):
                tc = (q4 * 512) // 128 + t4
                builders.append(lambda e, tc=tc, t4=t4, e_idx=e_idx: e.matmul(
                    out=pmr_ap()[0:1, t4 * 128:(t4 + 1) * 128],
                    lhsT=m_tmb[:, tc, e_idx:e_idx + 1], rhs=eyeb[:],
                    start=True, stop=True))
            sc.op("tensor", builders,
                  reads=[f"m_tm{(q4 * 512) // 128 + t4}" for t4 in range(nck)] + ["consts_eye"],
                  writes=[pmr_res])
            sc.op("vector", [lambda e, n=n: e.tensor_copy(out=mrow[0:1, 0:n],
                  in_=pmr_ap()[0:1, 0:n])], reads=[pmr_res], writes=["mrow"])
            sc.op("tensor", [lambda e, n=n: e.matmul(out=pbm_ap()[:, 0:n], lhsT=onesrowb[:],
                  rhs=mrow[0:1, 0:n], start=True, stop=True)],
                  reads=["mrow", "consts"], writes=[pbm_res])
            sc.op("vector", [lambda e, par=par, q4=q4, n=n: e.tensor_copy(
                out=m_rep[:, par, q4 * 512:q4 * 512 + n], in_=pbm_ap()[:, 0:n])],
                reads=[pbm_res], writes=[f"m_rep{par}"])

    build_mrep(0, lambda: pS[2][0:1, 0, 0:512], "bS20",
               lambda: pS[2][:, 1, 0:512], "bS21")
    if DBG:
        dma("sync", "dbg", d_xn, xn[:, :, :],
            reads=[f"xn{f}_{t}" for f in range(FC) for t in range(NT)])
        dma("sync", "dbg", d_m, m_tm[:, :, :],
            reads=[f"m_tm{tc}" for tc in range(NTC)])
        sc.wait_raw("sync", "dbg", sc.cnt["dbg"])
    flush_phase()
    esB.close()

    esD = ExitStack()
    sbD = lambda name, shape, dt: esD.enter_context(nc.sbuf_tensor(name, shape, dt))
    wg_sb = sbD("wgs", [128, 2, FC, SLAB], BF16)
    wu_sb = sbD("wus", [128, 2, FC, SLAB], BF16)
    wd_sb = sbD("wds", [128, 2, NKD, H], BF16)
    tbuf = sbD("tbuf", [128, 3, NMC, TAU], BF16)
    sgscr = sbD("sgscr", [128, 2, TAU], F32)
    umscr = sbD("umscr", [128, 2, TAU], F32)
    pg = esD.enter_context(nc.psum_tensor("pg", [128, 2, 512], F32))
    pu = esD.enter_context(nc.psum_tensor("pu", [128, 2, 512], F32))
    py = esD.enter_context(nc.psum_tensor("py", [128, 3, 512], F32))
    pbm = esD.enter_context(nc.psum_tensor("pbm", [128, 512], F32))

    def load_slab(e_idx, s, par, split_first=False):
        if e_idx < NE:
            srcs = [dwg[e_idx, :, s * SLAB:(s + 1) * SLAB],
                    dwu[e_idx, :, s * SLAB:(s + 1) * SLAB],
                    dwd[e_idx, s * SLAB:(s + 1) * SLAB, :]]
        else:
            srcs = [dsg[:, s * SLAB:(s + 1) * SLAB],
                    dsu[:, s * SLAB:(s + 1) * SLAB],
                    dsd[s * SLAB:(s + 1) * SLAB, :]]
        if split_first:
            dma_group("gpsimd", "dwf2", [
                (wg_sb[:, par, :, :], srcs[0].rearrange("(kc p) d -> p kc d", p=128),
                 (), [f"wg{par}"])])
            dma_group("gpsimd", f"dwf{par}", [
                (wu_sb[:, par, :, :], srcs[1].rearrange("(kc p) d -> p kc d", p=128),
                 (), [f"wu{par}"]),
                (wd_sb[:, par, :, :], srcs[2].rearrange("(kc p) h -> p kc h", p=128),
                 (), [f"wd{par}"]),
            ])
            return
        dma_group("gpsimd", f"dwf{par}", [
            (wg_sb[:, par, :, :], srcs[0].rearrange("(kc p) d -> p kc d", p=128),
             (), [f"wg{par}"]),
            (wu_sb[:, par, :, :], srcs[1].rearrange("(kc p) d -> p kc d", p=128),
             (), [f"wu{par}"]),
            (wd_sb[:, par, :, :], srcs[2].rearrange("(kc p) h -> p kc h", p=128),
             (), [f"wd{par}"]),
        ])

    dyr = dy.rearrange("(fc p) t -> p fc t", p=128)

    def emit_wd(e_idx, s, t, par, tpar):
        for hc in range(FC):
            ppar = hc % 3
            builders = []
            for kd in range(NKD):
                builders.append(lambda e, kd=kd, hc=hc, par=par, tpar=tpar, ppar=ppar: e.matmul(
                    out=py[:, ppar, 0:TAU],
                    lhsT=wd_sb[:, par, kd, hc * 128:(hc + 1) * 128],
                    rhs=tbuf[:, tpar, kd, :],
                    start=(kd == 0), stop=(kd == NKD - 1)))
            sc.op("tensor", builders,
                  reads=[f"wd{par}"] + [f"tbuf{tpar}_{kd}" for kd in range(NKD)],
                  writes=[f"py{ppar}"])
            sc.op("vector", [lambda e, hc=hc, t=t, ppar=ppar: e.tensor_tensor(
                out=x[:, hc, t * TAU:(t + 1) * TAU], in0=x[:, hc, t * TAU:(t + 1) * TAU],
                in1=py[:, ppar, 0:TAU], op=AluOpType.add)],
                reads=[f"py{ppar}", f"x{hc}_{t}"], writes=[f"x{hc}_{t}"])
            if e_idx == NE - 1 and s == NSL - 1 and t == NT - 1 and hc % 2 == 1:
                dma("sync", "dy", dyr[:, hc - 1:hc + 1, t * TAU:(t + 1) * TAU],
                    x[:, hc - 1:hc + 1, t * TAU:(t + 1) * TAU],
                    reads=[f"x{hc - 1}_{t}", f"x{hc}_{t}"])
        if e_idx == NE - 1 and s == NSL - 1 and t < NT - 1:
            dma("sync", "dy", dyr[:, :, t * TAU:(t + 1) * TAU],
                x[:, :, t * TAU:(t + 1) * TAU],
                reads=[f"x{f}_{t}" for f in range(FC)])

    load_slab(NE, 0, 0, split_first=True)

    pending = None
    gti = [0]
    e_seq = [NE] + list(range(NE))
    for pos, e_idx in enumerate(e_seq):
        for s in range(NSL):
            par = (pos * NSL + s) % 2
            for t in range(NT):
                tpar = gti[0] % 3
                gti[0] += 1
                for mc in range(NMC):
                    gpar = mc % 2
                    gb = []
                    ub = []
                    for kc in range(FC):
                        gb.append(lambda e, kc=kc, mc=mc, t=t, par=par, gpar=gpar: e.matmul(
                            out=pg[:, gpar, 0:TAU],
                            lhsT=wg_sb[:, par, kc, mc * 128:(mc + 1) * 128],
                            rhs=xn[:, kc, t * TAU:(t + 1) * TAU],
                            start=(kc == 0), stop=(kc == FC - 1)))
                        ub.append(lambda e, kc=kc, mc=mc, t=t, par=par, gpar=gpar: e.matmul(
                            out=pu[:, gpar, 0:TAU],
                            lhsT=wu_sb[:, par, kc, mc * 128:(mc + 1) * 128],
                            rhs=xn[:, kc, t * TAU:(t + 1) * TAU],
                            start=(kc == 0), stop=(kc == FC - 1)))
                    xnr = [f"xn{kc}_{t}" for kc in range(FC)]
                    sc.op("tensor", gb, reads=[f"wg{par}"] + xnr, writes=[f"pg{gpar}"])
                    sc.op("tensor", ub, reads=[f"wu{par}"] + xnr, writes=[f"pu{gpar}"])
                    sc.op("scalar", [lambda e, gpar=gpar: e.activation(
                        out=sgscr[:, gpar, :], in_=pg[:, gpar, 0:TAU], func=AF.Silu)],
                        reads=[f"pg{gpar}"], writes=[f"sgscr{gpar}"])
                    if e_idx < NE:
                        sc.op("vector", [lambda e, gpar=gpar, e_idx=e_idx, t=t: e.tensor_tensor(
                            out=umscr[:, gpar, :], in0=pu[:, gpar, 0:TAU],
                            in1=m_rep[:, e_idx % 2, t * TAU:(t + 1) * TAU],
                            op=AluOpType.mult)],
                            reads=[f"pu{gpar}", f"m_rep{e_idx % 2}"],
                            writes=[f"umscr{gpar}"])
                        sc.op("vector", [lambda e, gpar=gpar, tpar=tpar, mc=mc: e.tensor_tensor(
                            out=tbuf[:, tpar, mc, :], in0=sgscr[:, gpar, :],
                            in1=umscr[:, gpar, :], op=AluOpType.mult)],
                            reads=[f"sgscr{gpar}", f"umscr{gpar}"],
                            writes=[f"tbuf{tpar}_{mc}"])
                    else:
                        sc.op("vector", [lambda e, gpar=gpar, tpar=tpar, mc=mc: e.tensor_tensor(
                            out=tbuf[:, tpar, mc, :], in0=sgscr[:, gpar, :],
                            in1=pu[:, gpar, 0:TAU], op=AluOpType.mult)],
                            reads=[f"sgscr{gpar}", f"pu{gpar}"],
                            writes=[f"tbuf{tpar}_{mc}"])
                if pending is not None:
                    emit_wd(*pending)
                pending = (e_idx, s, t, par, tpar)
                if t == 0:
                    nxt = pos * NSL + s + 1
                    if nxt < (NE + 1) * NSL:
                        load_slab(e_seq[nxt // NSL], nxt % NSL, nxt % 2)
                    if s == 0 and pos + 1 < len(e_seq) and 1 <= e_seq[pos + 1] < NE:
                        build_mrep(e_seq[pos + 1], lambda: py[0:1, 2, 0:512], "py2",
                                   lambda: pbm[:, :], "pbm")
    emit_wd(*pending)
    sc.wait_raw("sync", "dy", sc.cnt["dy"])
    flush_phase()
    esD.close()
    es0.close()
    return nc


def _host_prep(L0_attn_X, L0_attn_W_qkv, L0_attn_b_qkv, L0_attn_W_out, L0_attn_b_out,
               L0_ffn_Wr, L0_ffn_Wg, L0_ffn_Wu, L0_ffn_Wd, L0_ffn_Sg, L0_ffn_Su, L0_ffn_Sd,
               n_cores=8):
    f32 = np.float32
    X = np.asarray(L0_attn_X, f32)
    Wqkv = np.asarray(L0_attn_W_qkv, f32)
    bqkv = np.asarray(L0_attn_b_qkv, f32)
    Wout = np.asarray(L0_attn_W_out, f32)
    bout = np.asarray(L0_attn_b_out, f32)
    Wr = np.asarray(L0_ffn_Wr, f32)
    Wg = np.asarray(L0_ffn_Wg, f32)
    Wu = np.asarray(L0_ffn_Wu, f32)
    Wd = np.asarray(L0_ffn_Wd, f32)
    Sg = np.asarray(L0_ffn_Sg, f32)
    Su = np.asarray(L0_ffn_Su, f32)
    Sd = np.asarray(L0_ffn_Sd, f32)

    perm = []
    for hp in range(8):
        h0, h1 = 2 * hp, 2 * hp + 1
        perm += list(range(h0 * 192, h0 * 192 + 64)) + list(range(h1 * 192, h1 * 192 + 64))
        perm += list(range(h0 * 192 + 64, h0 * 192 + 128)) + list(range(h1 * 192 + 64, h1 * 192 + 128))
        perm += list(range(h0 * 192 + 128, h0 * 192 + 192)) + list(range(h1 * 192 + 128, h1 * 192 + 192))
    perm = np.asarray(perm)
    wqkv_p = np.ascontiguousarray(Wqkv[:, perm])
    bqkv_p = np.ascontiguousarray(bqkv[perm].reshape(24, 128).T)
    bout_t = np.ascontiguousarray(bout.reshape(8, 128).T)
    wr_t = np.ascontiguousarray(Wr.reshape(8, 128, 8).transpose(1, 0, 2))

    common = dict(
        wqkv=wqkv_p, bqkv=bqkv_p, wout=np.ascontiguousarray(Wout), bout=bout_t,
        wr=wr_t, wg=np.ascontiguousarray(Wg), wu=np.ascontiguousarray(Wu),
        wd=np.ascontiguousarray(Wd), sg=np.ascontiguousarray(Sg),
        su=np.ascontiguousarray(Su), sd=np.ascontiguousarray(Sd),
        eye=np.eye(128, dtype=f32),
        eyeb=np.eye(128, dtype=ml_dtypes.bfloat16),
        onesrow=np.ones((1, 128), dtype=f32),
        onesrowb=np.ones((1, 128), dtype=ml_dtypes.bfloat16),
        onescol=np.ones((128, 1), dtype=ml_dtypes.bfloat16),
    )
    in_maps = []
    for c in range(n_cores):
        m = dict(common)
        m["x_in"] = np.ascontiguousarray(X[c].T)
        in_maps.append(m)
    return in_maps


def kernel(**inputs):
    in_maps = _host_prep(**inputs)
    cfg = dict(T=2048, TAU=512, DFF=2048, SLAB=512)
    nc = build_nc(cfg)
    res = run_bass_kernel_spmd(nc, in_maps, core_ids=list(range(8)))
    outs = [np.asarray(r["y"], dtype=np.float32).T for r in res.results]
    return np.stack(outs, axis=0)
